# version 04 (final)
# baseline (speedup 1.0000x reference)
import numpy as np
import ml_dtypes
import concourse.bass as bass
import concourse.mybir as mybir
from concourse.bass_utils import run_bass_kernel_spmd

F32 = mybir.dt.float32
BF16 = mybir.dt.bfloat16
U8 = mybir.dt.uint8
AF = mybir.ActivationFunctionType
ALU = mybir.AluOpType
AX = mybir.AxisListType

B, S, D = 2, 4096, 4096
H, DN, DVd, DR = 32, 128, 128, 64
DQ = DN + DR
HID = 11008
RK = 16
SCALING = 2.0
QSCALE = float(DQ) ** -0.5
EPS = 1e-5
NCORE = 8
TOK = 1024
DT = D // 128
OT = HID // 128
L0, L1 = 16, 32
ENGS = ("sync", "gpsimd", "scalar", "vector", "tensor")

ARENA_BYTES = 141 * 1024
P2A_FG = 8
P2A_TC = 8
SKIP = set()

Z = ("sLD", 0)


def T3(ap3):
    return ap3.transpose([1, 0, 2])


def V3(sb2, k, c):
    return sb2.rearrange("p (k c) -> p k c", k=k)


class Plan:

    def __init__(self, nc, sems):
        self.nc = nc
        self.sems = sems
        self.cnt = {k: 0 for k in sems}
        self.ops = {e: [] for e in ENGS}

    def emit(self, eng, fn):
        self.ops[eng].append(fn)

    def wait(self, eng, guard):
        sem, val = guard
        if val <= 0:
            return
        s = self.sems[sem]
        self.emit(eng, lambda e, s=s, v=val: e.wait_ge(s, v))

    def inc(self, sem, n):
        self.cnt[sem] += n
        return (sem, self.cnt[sem])

    def dma(self, eng, sem, out, in_):
        g = self.inc(sem, 16)
        s = self.sems[sem]

        def fn(e, out=out, in_=in_, s=s):
            e.dma_start(out=out, in_=in_).then_inc(s, 16)

        self.emit(eng, fn)
        return g

    def mm(self, out, lhsT, rhs, start, stop):
        def fn(e, out=out, lhsT=lhsT, rhs=rhs, a=start, b=stop):
            return e.matmul(out, lhsT, rhs, start=a, stop=b)

        self.emit("tensor", fn)

    def mm_done(self, fence=None):
        s = self.sems["sMM"]
        if fence is not None:
            def ffn(e, b=fence):
                return e.matmul(b[0:16, 0:512], self.zer, self.zrow,
                                start=False, stop=True)
            self.emit("tensor", ffn)
        g = self.inc("sMM", 1)
        prev = self.ops["tensor"].pop()

        def fn2(e, prev=prev, s=s):
            prev(e).then_inc(s, 1)

        self.emit("tensor", fn2)
        return g

    def op(self, eng, sem, fn):
        g = self.inc(sem, 1) if sem else None
        s = self.sems[sem] if sem else None

        def wrap(e, fn=fn, s=s):
            i = fn(e)
            if s is not None:
                i.then_inc(s, 1)

        self.emit(eng, wrap)
        return g

    def opc(self, eng, sem, fn):
        g = self.op(eng, sem, fn)
        self.wait(eng, g)
        return g

    def barrier(self):
        vals = dict(self.cnt)
        for e in ENGS:
            for k, v in vals.items():
                self.wait(e, (k, v))


def build_nc(phases=6, taps=()):
    nc = bass.Bass("TRN2", target_bir_lowering=False)

    x_seq = nc.dram_tensor("x_seq", [S, D], F32, kind="ExternalInput").ap()
    x_own = nc.dram_tensor("x_own", [TOK, D], F32, kind="ExternalInput").ap()
    wq = nc.dram_tensor("wq", [128, DT, H * DN], F32, kind="ExternalInput").ap()
    wk = nc.dram_tensor("wk", [128, DT, H * DN], F32, kind="ExternalInput").ap()
    wv = nc.dram_tensor("wv", [128, DT, H * DVd], F32, kind="ExternalInput").ap()
    wo = nc.dram_tensor("wo", [128, DT, D], F32, kind="ExternalInput").ap()
    w1t = nc.dram_tensor("w1t", [128, DT, HID], F32, kind="ExternalInput").ap()
    w3t = nc.dram_tensor("w3t", [128, DT, HID], F32, kind="ExternalInput").ap()
    w2t = nc.dram_tensor("w2t", [128, OT, D], F32, kind="ExternalInput").ap()
    a1t = nc.dram_tensor("a1t", [128, DT, RK], F32, kind="ExternalInput").ap()
    a3t = nc.dram_tensor("a3t", [128, DT, RK], F32, kind="ExternalInput").ap()
    a2t = nc.dram_tensor("a2t", [128, OT, RK], F32, kind="ExternalInput").ap()
    b1t = nc.dram_tensor("b1t", [RK, HID], F32, kind="ExternalInput").ap()
    b3t = nc.dram_tensor("b3t", [RK, HID], F32, kind="ExternalInput").ap()
    b2t = nc.dram_tensor("b2t", [RK, D], F32, kind="ExternalInput").ap()
    mskin = nc.dram_tensor("mskin", [128, L0 + L1, 512], F32,
                           kind="ExternalInput").ap()
    identin = nc.dram_tensor("identin", [128, 128], F32,
                             kind="ExternalInput").ap()
    onesin = nc.dram_tensor("onesin", [128, 1], F32, kind="ExternalInput").ap()
    onesrin = nc.dram_tensor("onesrin", [1, 128], F32, kind="ExternalInput").ap()

    out = nc.dram_tensor("out", [TOK, D], F32, kind="ExternalOutput").ap()

    xnfs = nc.dram_tensor("xnfs", [128, DT, S], BF16, kind="Internal").ap()
    xnfo = nc.dram_tensor("xnfo", [128, DT, TOK], BF16, kind="Internal").ap()
    kfm = nc.dram_tensor("kfm", [128, DT, S], BF16, kind="Internal").ap()
    vtm = nc.dram_tensor("vtm", [S // 128, 128, H * DVd], BF16,
                         kind="Internal").ap()
    qf = nc.dram_tensor("qf", [128, DT, TOK], BF16, kind="Internal").ap()
    gbuf = nc.dram_tensor("gbuf", [OT, 128, TOK], BF16, kind="Internal").ap()
    opart = nc.dram_tensor("opart", [TOK, D], F32, kind="Internal").ap()

    srcs = {"xnfs": xnfs, "xnfo": xnfo, "kfm": kfm, "vtm": vtm, "qf": qf,
            "gbuf": gbuf, "opart": opart}
    attn_tap = None
    if "attn" in taps:
        attn_tap = nc.dram_tensor("tap_attn", [128, H * TOK], mybir.dt.uint16,
                                  kind="ExternalOutput").ap()
    U16 = mybir.dt.uint16
    tap_outs = {}
    for t in taps:
        if t == "attn":
            continue
        src = srcs[t]
        dt = F32 if t == "opart" else U16
        tap_outs[t] = nc.dram_tensor("tap_" + t, list(src.shape), dt,
                                     kind="ExternalOutput").ap()

    import contextlib
    _es = contextlib.ExitStack()
    arena_t = _es.enter_context(nc.sbuf_tensor("arena", [128, ARENA_BYTES], U8))
    ident = _es.enter_context(nc.sbuf_tensor("ident", [128, 128], BF16))
    ones = _es.enter_context(nc.sbuf_tensor("ones", [128, 1], BF16))
    onesr = _es.enter_context(nc.sbuf_tensor("onesr", [1, 128], BF16))
    st_nsum = _es.enter_context(nc.sbuf_tensor("st_nsum", [128, 1], F32))
    st_ssq = _es.enter_context(nc.sbuf_tensor("st_ssq", [128, 1], F32))
    st_nmean = _es.enter_context(nc.sbuf_tensor("st_nmean", [128, 1], F32))
    st_m2 = _es.enter_context(nc.sbuf_tensor("st_m2", [128, 1], F32))
    st_var = _es.enter_context(nc.sbuf_tensor("st_var", [128, 1], F32))
    st_rstd = _es.enter_context(nc.sbuf_tensor("st_rstd", [128, 1], F32))
    st_nbias = _es.enter_context(nc.sbuf_tensor("st_nbias", [128, 1], F32))
    s1sb = _es.enter_context(nc.sbuf_tensor("s1sb", [RK, TOK], BF16))
    s3sb = _es.enter_context(nc.sbuf_tensor("s3sb", [RK, TOK], BF16))
    s2sb = _es.enter_context(nc.sbuf_tensor("s2sb", [RK, TOK], BF16))
    b1sb = _es.enter_context(nc.sbuf_tensor("b1sb", [RK, HID], BF16))
    b3sb = _es.enter_context(nc.sbuf_tensor("b3sb", [RK, HID], BF16))
    b2sb = _es.enter_context(nc.sbuf_tensor("b2sb", [RK, D], BF16))
    a1sb = _es.enter_context(nc.sbuf_tensor("a1sb", [128, DT * RK], BF16))
    a3sb = _es.enter_context(nc.sbuf_tensor("a3sb", [128, DT * RK], BF16))
    a2sb = _es.enter_context(nc.sbuf_tensor("a2sb", [128, OT * RK], BF16))
    rsb = _es.enter_context(nc.sbuf_tensor("rsb", [1, 512], BF16))
    rsf = _es.enter_context(nc.sbuf_tensor("rsf", [1, 512], F32))
    zer = _es.enter_context(nc.sbuf_tensor("zer", [1, 16], BF16))
    zrow = _es.enter_context(nc.sbuf_tensor("zrow", [1, 512], BF16))
    pb0 = _es.enter_context(nc.psum_tensor("pb0", [128, 512], F32))
    pb1 = _es.enter_context(nc.psum_tensor("pb1", [128, 512], F32))
    pb2 = _es.enter_context(nc.psum_tensor("pb2", [128, 512], F32))
    pb3 = _es.enter_context(nc.psum_tensor("pb3", [128, 512], F32))
    pb4 = _es.enter_context(nc.psum_tensor("pb4", [128, 512], F32))
    pb5 = _es.enter_context(nc.psum_tensor("pb5", [128, 512], F32))
    pb6 = _es.enter_context(nc.psum_tensor("pb6", [128, 512], F32))
    pb7 = _es.enter_context(nc.psum_tensor("pb7", [128, 512], F32))
    _sLD = _es.enter_context(nc.semaphore("sLD"))
    _sGL = _es.enter_context(nc.semaphore("sGL"))
    _sGS = _es.enter_context(nc.semaphore("sGS"))
    _sMM = _es.enter_context(nc.semaphore("sMM"))
    _sDV = _es.enter_context(nc.semaphore("sDV"))
    _sAC = _es.enter_context(nc.semaphore("sAC"))
    _sOUT = _es.enter_context(nc.semaphore("sOUT"))
    block = _es.enter_context(nc.Block())
    with _es:
        sems = dict(sLD=_sLD, sGL=_sGL, sGS=_sGS, sMM=_sMM, sDV=_sDV,
                    sAC=_sAC, sOUT=_sOUT)
        P = Plan(nc, sems)
        P.zer = zer[0:1, 0:16]
        P.zrow = zrow[0:1, 0:512]
        P.onesr = onesr[0:1, 0:128]
        pb = [pb0, pb1, pb2, pb3, pb4, pb5, pb6, pb7]

        def ar(off_bytes, parts, n, dtype):
            esz = {F32: 4, BF16: 2}[dtype]
            assert off_bytes % 32 == 0
            assert off_bytes + n * esz <= ARENA_BYTES
            return arena_t[0:parts, off_bytes:off_bytes + n * esz].bitcast(dtype)

        P.op("vector", "sDV", lambda e: e.memset(zer[:, :], 0.0))
        g_z = P.op("vector", "sDV", lambda e: e.memset(zrow[:, :], 0.0))
        P.wait("tensor", g_z)
        P.dma("gpsimd", "sGL", ident[:, :], identin[:, :])
        P.dma("gpsimd", "sGL", ones[:, :], onesin[:, :])
        g_consts = P.dma("gpsimd", "sGL", onesr[:, :], onesrin[:, :])
        P.wait("tensor", g_consts)

        if phases >= 1:
            xsb = [ar(0 + i * 16384, 128, 4096, F32) for i in range(2)]
            sqt = [ar(32768 + i * 16384, 128, 4096, F32) for i in range(2)]
            xnb = [ar(65536 + i * 8192, 128, 4096, BF16) for i in range(2)]
            tsb = [ar(81920 + i * 2048, 128, 1024, BF16) for i in range(2)]
            g_x_free = [Z, Z]
            g_sq_free = [Z, Z]
            g_xnb_free = [Z, Z]
            g_tsb_free = [Z, Z]
            g_pt_free = Z
            for i in range(40):
                bi = i % 2
                if i < 32:
                    src = x_seq[128 * i:128 * (i + 1), :]
                    dstt, col = xnfs, 128 * i
                else:
                    src = x_own[128 * (i - 32):128 * (i - 31), :]
                    dstt, col = xnfo, 128 * (i - 32)
                P.wait("sync", g_x_free[bi])
                g_x = P.dma("sync", "sLD", xsb[bi][:, :], src)
                P.wait("vector", g_x)
                P.opc("vector", "sDV", lambda e, s=xsb[bi]: e.reduce_sum(
                    st_nsum[:, :], s[:, :], axis=AX.X, negate=True))
                P.wait("scalar", g_x)
                P.wait("scalar", g_sq_free[bi])
                g_sq = P.op("scalar", "sAC", lambda e, o=sqt[bi], s=xsb[bi]:
                            e.activation(o[:, :], s[:, :], AF.Square))
                P.wait("vector", g_sq)
                g_ssq = P.opc("vector", "sDV", lambda e, s=sqt[bi]:
                              e.reduce_sum(st_ssq[:, :], s[:, :], axis=AX.X))
                g_sq_free[bi] = g_ssq
                P.opc("vector", "sDV", lambda e: e.tensor_scalar_mul(
                    st_nmean[:, :], st_nsum[:, :], 1.0 / D))
                P.opc("vector", "sDV", lambda e: e.tensor_tensor(
                    st_m2[:, :], st_nmean[:, :], st_nmean[:, :], ALU.mult))
                P.opc("vector", "sDV", lambda e: e.tensor_scalar_mul(
                    st_var[:, :], st_ssq[:, :], 1.0 / D))
                g_var = P.opc("vector", "sDV", lambda e: e.scalar_tensor_tensor(
                    st_var[:, :], st_var[:, :], EPS, st_m2[:, :],
                    ALU.add, ALU.subtract))
                P.wait("scalar", g_var)
                P.opc("scalar", "sAC", lambda e: e.activation(
                    st_rstd[:, :], st_var[:, :], AF.Ln))
                g_rstd = P.op("scalar", "sAC", lambda e: e.activation(
                    st_rstd[:, :], st_rstd[:, :], AF.Exp, scale=-0.5))
                P.wait("vector", g_rstd)
                P.opc("vector", "sDV", lambda e: e.tensor_tensor(
                    st_nbias[:, :], st_nmean[:, :], st_rstd[:, :], ALU.mult))
                P.wait("vector", g_xnb_free[bi])
                g_xn = P.op("vector", "sDV", lambda e, o=xnb[bi], s=xsb[bi]:
                            e.tensor_scalar(o[:, :], s[:, :], st_rstd[:, :],
                                            st_nbias[:, :], ALU.mult, ALU.add))
                g_x_free[bi] = g_xn
                for rnd in range(4):
                    P.wait("tensor", g_xn)
                    P.wait("tensor", g_pt_free)
                    for j in range(8):
                        k = rnd * 8 + j
                        pv = pb[j][:, 0:64].bitcast(BF16)
                        P.emit("tensor",
                               lambda e, o=pv,
                               s=xnb[bi][:, 128 * k:128 * (k + 1)]:
                               e.transpose(o, s, ident[:, :]))
                    g_tr = P.mm_done()
                    if rnd == 3:
                        g_xnb_free[bi] = g_tr
                    tbi = (i * 4 + rnd) % 2
                    P.wait("vector", g_tr)
                    P.wait("vector", g_tsb_free[tbi])
                    for j in range(8):
                        pv = pb[j][:, 0:64].bitcast(BF16)
                        g_ev = P.op("vector", "sDV" if j == 7 else None,
                                    lambda e, o=tsb[tbi][:, 128 * j:128 * (j + 1)],
                                    s=pv: e.tensor_copy(o, s))
                    g_pt_free = g_ev
                    dst = dstt[:, rnd * 8:rnd * 8 + 8, col:col + 128]
                    P.wait("gpsimd", g_ev)
                    g_tsb_free[tbi] = P.dma("gpsimd", "sGS", dst,
                                            V3(tsb[tbi][:, :], 8, 128))
            P.barrier()

        if phases >= 1.5:
            wkb = ar(0, 128, DT * 512, BF16)
            wvb = ar(32768, 128, DT * 512, BF16)
            xnsb = [ar(65536 + i * 32768, 128, DT * 512, BF16) for i in range(2)]
            stg = [ar(131072 + i * 1024, 128, 512, BF16) for i in range(8)]
            g_stg_free = [Z] * 8
            g_xn_free = [Z, Z]
            g_w_free = Z
            g_bank_free = [Z] * 4
            sg = 0
            for fg in range(P2A_FG):
                P.wait("gpsimd", g_w_free)
                s1_ = wk[:, :, 512 * fg:512 * fg + 512]
                s2_ = wv[:, :, 512 * fg:512 * fg + 512]
                for half in range(2):
                    P.dma("gpsimd", "sGL",
                          V3(wkb, DT, 512)[:, 16 * half:16 * (half + 1), :],
                          s1_[:, 16 * half:16 * (half + 1), :])
                    g_w = P.dma("gpsimd", "sGL",
                                V3(wvb, DT, 512)[:, 16 * half:16 * (half + 1), :],
                                s2_[:, 16 * half:16 * (half + 1), :])
                for tc in range(P2A_TC):
                    bi = tc % 2
                    P.wait("sync", g_xn_free[bi])
                    for q in range(4):
                        s3_ = xnfs[:, 8 * q:8 * q + 8, 512 * tc:512 * tc + 512]
                        g_xn = P.dma("sync", "sLD",
                                     V3(xnsb[bi], DT, 512)[:, 8 * q:8 * (q + 1), :],
                                     s3_)
                    P.wait("tensor", g_xn)
                    P.wait("tensor", g_w)
                    for f in range(4 if "K" not in SKIP else 0):
                        bk = f % 2
                        P.wait("tensor", g_bank_free[bk])
                        for d in range(DT):
                            P.mm(pb[bk][:, :],
                                 wkb[:, d * 512 + f * 128:d * 512 + (f + 1) * 128],
                                 xnsb[bi][:, d * 512:(d + 1) * 512],
                                 start=(d == 0), stop=(d == DT - 1))
                        g_mm = P.mm_done(fence=pb[bk])
                        P.wait("vector", g_mm)
                        P.wait("vector", g_stg_free[sg % 8])
                        g_ev = P.op("vector", "sDV",
                                    lambda e, o=stg[sg % 8], s=pb[bk]:
                                    e.tensor_copy(o[:, :], s[:, :]))
                        g_bank_free[bk] = g_ev
                        P.wait("gpsimd", g_ev)
                        g_stg_free[sg % 8] = P.dma(
                            "gpsimd", "sGS",
                            kfm[:, 4 * fg + f, 512 * tc:512 * tc + 512],
                            stg[sg % 8][:, :])
                        sg += 1
                    for t in range(4 if "V" not in SKIP else 0):
                        bk = 2 + t % 2
                        P.wait("tensor", g_bank_free[bk])
                        for d in range(DT):
                            P.mm(pb[bk][:, :],
                                 xnsb[bi][:, d * 512 + t * 128:d * 512 + (t + 1) * 128],
                                 wvb[:, d * 512:(d + 1) * 512],
                                 start=(d == 0), stop=(d == DT - 1))
                        g_mm = P.mm_done(fence=pb[bk])
                        P.wait("vector", g_mm)
                        P.wait("vector", g_stg_free[sg % 8])
                        g_ev = P.op("vector", "sDV",
                                    lambda e, o=stg[sg % 8], s=pb[bk]:
                                    e.tensor_copy(o[:, :], s[:, :]))
                        g_bank_free[bk] = g_ev
                        P.wait("gpsimd", g_ev)
                        g_stg_free[sg % 8] = P.dma(
                            "gpsimd", "sGS",
                            vtm[4 * tc + t, :, 512 * fg:512 * fg + 512],
                            stg[sg % 8][:, :])
                        sg += 1
                    if "K" not in SKIP or "V" not in SKIP:
                        g_xn_free[bi] = g_mm
                if "K" not in SKIP or "V" not in SKIP:
                    g_w_free = g_mm
            P.barrier()

        if phases >= 2:
            xosb = ar(65536, 128, DT * TOK, BF16)
            wqb = ar(0, 128, DT * 512, BF16)
            stg = [ar(131072 + i * 1024, 128, 512, BF16) for i in range(8)]
            g_stg_free = [Z] * 8
            g_bank_free = [Z, Z]
            for q in range(4):
                s_ = xnfo[:, 8 * q:8 * q + 8, :]
                g_xo = P.dma("sync", "sLD",
                             V3(xosb, DT, TOK)[:, 8 * q:8 * (q + 1), :], s_)
            P.dma("gpsimd", "sGL", V3(a1sb[:, :], DT, RK), a1t[:, :, :])
            P.dma("gpsimd", "sGL", V3(a2sb[:, :], OT, RK), a2t[:, :, :])
            g_a = P.dma("gpsimd", "sGL", V3(a3sb[:, :], DT, RK), a3t[:, :, :])
            sg = 0
            g_w_free = Z
            for fg in range(8):
                P.wait("gpsimd", g_w_free)
                s_ = wq[:, :, 512 * fg:512 * fg + 512]
                for half in range(2):
                    g_w = P.dma("gpsimd", "sGL",
                                V3(wqb, DT, 512)[:, 16 * half:16 * (half + 1), :],
                                s_[:, 16 * half:16 * (half + 1), :])
                P.wait("tensor", g_w)
                P.wait("tensor", g_xo)
                for f in range(4):
                    for tc in range(2):
                        bk = (f * 2 + tc) % 2
                        P.wait("tensor", g_bank_free[bk])
                        for d in range(DT):
                            P.mm(pb[bk][:, :],
                                 wqb[:, d * 512 + f * 128:d * 512 + (f + 1) * 128],
                                 xosb[:, d * TOK + tc * 512:d * TOK + (tc + 1) * 512],
                                 start=(d == 0), stop=(d == DT - 1))
                        g_mm = P.mm_done(fence=pb[bk])
                        P.wait("scalar", g_mm)
                        P.wait("scalar", g_stg_free[sg % 8])
                        g_ev = P.op("scalar", "sAC",
                                    lambda e, o=stg[sg % 8], s=pb[bk]:
                                    e.activation(o[:, :], s[:, :], AF.Copy,
                                                 scale=QSCALE))
                        g_bank_free[bk] = g_ev
                        P.wait("gpsimd", g_ev)
                        g_stg_free[sg % 8] = P.dma(
                            "gpsimd", "sGS",
                            qf[:, 4 * fg + f, 512 * tc:512 * tc + 512],
                            stg[sg % 8][:, :])
                        sg += 1
                g_w_free = g_mm
            P.wait("tensor", g_a)
            g_l_free = Z
            for asb, ssb in ((a1sb, s1sb), (a3sb, s3sb)):
                for tc in range(2):
                    P.wait("tensor", g_l_free)
                    for d in range(DT):
                        P.mm(pb6[0:RK, :],
                             asb[:, d * RK:(d + 1) * RK],
                             xosb[:, d * TOK + tc * 512:d * TOK + (tc + 1) * 512],
                             start=(d == 0), stop=(d == DT - 1))
                    g_mm = P.mm_done(fence=pb6)
                    P.wait("scalar", g_mm)
                    g_l_free = P.op("scalar", "sAC",
                                    lambda e, o=ssb, tc=tc: e.activation(
                                        o[0:RK, 512 * tc:512 * (tc + 1)],
                                        pb6[0:RK, :], AF.Copy, scale=SCALING))
            P.barrier()

        if phases >= 3:
            xosb = ar(65536, 128, DT * TOK, BF16)
            w1b = [ar(0 + i * 16384, 128, DT * 256, BF16) for i in range(2)]
            w3b = [ar(32768 + i * 16384, 128, DT * 256, BF16) for i in range(2)]
            silub = [ar(131072 + i * 1024, 128, 512, BF16) for i in range(2)]
            gsb = [ar(133120 + i * 1024, 128, 512, BF16) for i in range(4)]
            P.dma("gpsimd", "sGL", b1sb[:, :], b1t[:, :])
            g_b = P.dma("gpsimd", "sGL", b3sb[:, :], b3t[:, :])
            g_w_free = [Z, Z]
            g_h1_free = [Z, Z]
            g_h3_free = [Z, Z]
            g_silu_free = [Z, Z]
            g_gsb_free = [Z] * 4
            for oc in range(OT // 2):
                bi = oc % 2
                P.wait("gpsimd", g_w_free[bi])
                s1_ = w1t[:, :, 256 * oc:256 * oc + 256]
                s3_ = w3t[:, :, 256 * oc:256 * oc + 256]
                for q in range(4):
                    P.dma("gpsimd", "sGL",
                          V3(w1b[bi], DT, 256)[:, 8 * q:8 * (q + 1), :],
                          s1_[:, 8 * q:8 * (q + 1), :])
                    g_w = P.dma("gpsimd", "sGL",
                                V3(w3b[bi], DT, 256)[:, 8 * q:8 * (q + 1), :],
                                s3_[:, 8 * q:8 * (q + 1), :])
                P.wait("tensor", g_w)
                if oc == 0:
                    P.wait("tensor", g_b)
                for osub in range(2):
                    o = oc * 2 + osub
                    for tc in range(2):
                        rot = (o * 2 + tc) % 2
                        bh1 = pb[0 + rot * 4]
                        bh3 = pb[1 + rot * 4]
                        gi = (o * 2 + tc) % 4
                        P.wait("tensor", g_h1_free[rot])
                        for d in range(DT):
                            P.mm(bh1[:, :],
                                 w1b[bi][:, d * 256 + osub * 128:
                                         d * 256 + (osub + 1) * 128],
                                 xosb[:, d * TOK + tc * 512:d * TOK + (tc + 1) * 512],
                                 start=(d == 0), stop=False)
                        P.mm(bh1[:, :], b1sb[:, o * 128:(o + 1) * 128],
                             s1sb[:, tc * 512:(tc + 1) * 512],
                             start=False, stop=True)
                        g_h1 = P.mm_done(fence=bh1)
                        P.wait("tensor", g_h3_free[rot])
                        for d in range(DT):
                            P.mm(bh3[:, :],
                                 w3b[bi][:, d * 256 + osub * 128:
                                         d * 256 + (osub + 1) * 128],
                                 xosb[:, d * TOK + tc * 512:d * TOK + (tc + 1) * 512],
                                 start=(d == 0), stop=False)
                        P.mm(bh3[:, :], b3sb[:, o * 128:(o + 1) * 128],
                             s3sb[:, tc * 512:(tc + 1) * 512],
                             start=False, stop=True)
                        g_h3 = P.mm_done(fence=bh3)
                        P.wait("scalar", g_h1)
                        P.wait("scalar", g_silu_free[rot])
                        g_si = P.op("scalar", "sAC",
                                    lambda e, o_=silub[rot], s=bh1:
                                    e.activation(o_[:, :], s[:, :], AF.Silu))
                        g_h1_free[rot] = g_si
                        P.wait("vector", g_si)
                        P.wait("vector", g_h3)
                        P.wait("vector", g_gsb_free[gi])
                        g_g = P.op("vector", "sDV",
                                   lambda e, o_=gsb[gi], a=bh3, b=silub[rot]:
                                   e.tensor_tensor(o_[:, :], a[:, :], b[:, :],
                                                   ALU.mult))
                        g_h3_free[rot] = g_g
                        g_silu_free[rot] = g_g
                        P.wait("gpsimd", g_g)
                        g_gsb_free[gi] = P.dma(
                            "gpsimd", "sGS",
                            gbuf[o, :, 512 * tc:512 * tc + 512], gsb[gi][:, :])
                g_w_free[bi] = g_h3
            P.barrier()
            g2b = [ar(0 + i * 2048, 128, TOK, BF16) for i in range(2)]
            g_g2_free = [Z, Z]
            for o in range(OT):
                bi = o % 2
                P.wait("sync", g_g2_free[bi])
                g_g2 = P.dma("sync", "sLD", g2b[bi][:, :], gbuf[o, :, :])
                P.wait("tensor", g_g2)
                P.mm(pb6[0:RK, :], a2sb[:, o * RK:(o + 1) * RK],
                     g2b[bi][:, 0:512], start=(o == 0), stop=(o == OT - 1))
                P.mm(pb2[0:RK, :], a2sb[:, o * RK:(o + 1) * RK],
                     g2b[bi][:, 512:1024], start=(o == 0), stop=(o == OT - 1))
                g_g2_free[bi] = P.mm_done(fence=(pb2 if o == OT - 1 else None))
            g_mm = g_g2_free[(OT - 1) % 2]
            P.wait("scalar", g_mm)
            P.op("scalar", "sAC", lambda e: e.activation(
                s2sb[0:RK, 0:512], pb6[0:RK, :], AF.Copy, scale=SCALING))
            P.op("scalar", "sAC", lambda e: e.activation(
                s2sb[0:RK, 512:1024], pb2[0:RK, :], AF.Copy, scale=SCALING))
            P.barrier()

        if phases >= 4:
            attn = ar(65536, 128, H * TOK, BF16)
            mskb = ar(0, 128, L1 * 512, BF16)
            qsb = [ar(32768 + i * 2048, 128, 2 * 512, BF16) for i in range(2)]
            ksb = [ar(36864 + i * 512, 128, 2 * 128, BF16) for i in range(2)]
            vsb = [ar(37888 + i * 512, 128, 2 * 128, BF16) for i in range(2)]
            prb = [ar(38912 + i * 1024, 128, 512, BF16) for i in range(4)]
            pmb = [ar(43008 + i * 1024, 128, 512, BF16) for i in range(4)]
            bcb = [ar(47104 + i * 2048, 128, 512, F32) for i in range(2)]
            SC = [2, 3, 7]
            g_q_free = [Z, Z]
            g_kv_free = [Z, Z]
            g_pr_free = [Z] * 4
            g_pm_free = [Z] * 4
            g_sc_free = [Z] * 3
            g_pv_free = [Z, Z]
            g_sum_free = [Z, Z]
            g_bc_free = [Z, Z]
            g_bc_bank_free = Z
            g_rsb_free = Z
            g_msk_free = Z
            it = 0
            for slot in range(2):
                LS = L0 if slot == 0 else L1
                P.wait("gpsimd", g_msk_free)
                base = 0 if slot == 0 else L0
                for t4 in range(LS // 4):
                    g_mk = P.dma("gpsimd", "sGL",
                                 V3(mskb, L1, 512)[:, 4 * t4:4 * (t4 + 1), :],
                                 mskin[:, base + 4 * t4:base + 4 * (t4 + 1), :])
                for hg in range(16):
                    bq = hg % 2
                    P.wait("sync", g_q_free[bq])
                    s_ = qf[:, 2 * hg:2 * hg + 2, 512 * slot:512 * slot + 512]
                    g_q = P.dma("sync", "sLD", V3(qsb[bq], 2, 512), s_)
                    g_last_scores = Z
                    for t in range(LS):
                        bkv = t % 2
                        P.wait("sync", g_kv_free[bkv])
                        s_ = kfm[:, 2 * hg:2 * hg + 2, 128 * t:128 * t + 128]
                        P.dma("sync", "sLD", V3(ksb[bkv], 2, 128), s_)
                        g_kv = P.dma("sync", "sLD", vsb[bkv][:, :],
                                     vtm[t, :, 256 * hg:256 * hg + 256])
                        for h in range(2):
                            sb = it % 3
                            pr_i = it % 4
                            bank = pb[SC[sb]]
                            P.wait("tensor", g_kv)
                            P.wait("tensor", g_q)
                            if slot == 0 and hg == 0 and t == 0:
                                P.wait("tensor", g_mk)
                            P.wait("tensor", g_sc_free[sb])
                            if t == 0:
                                P.wait("tensor", g_pv_free[h])
                                P.wait("tensor", g_sum_free[h])
                            P.mm(bank[:, :],
                                 ksb[bkv][:, h * 128:(h + 1) * 128],
                                 qsb[bq][:, h * 512:(h + 1) * 512],
                                 start=True, stop=True)
                            g_s = P.mm_done(fence=bank)
                            g_last_scores = g_s
                            P.wait("scalar", g_s)
                            P.wait("scalar", g_pr_free[pr_i])
                            g_e = P.op("scalar", "sAC",
                                       lambda e, o=prb[pr_i], s=bank:
                                       e.activation(o[:, :], s[:, :], AF.Exp))
                            g_sc_free[sb] = g_e
                            P.wait("vector", g_e)
                            if hg == 0 and t == 0 and h == 0:
                                P.wait("vector", g_mk)
                            P.wait("vector", g_pm_free[pr_i])
                            g_m = P.op("vector", "sDV",
                                       lambda e, o=pmb[pr_i], a=prb[pr_i],
                                       b=mskb[:, 512 * t:512 * (t + 1)]:
                                       e.tensor_tensor(o[:, :], a[:, :],
                                                       b, ALU.mult))
                            g_pr_free[pr_i] = g_m
                            P.wait("tensor", g_m)
                            P.mm(pb[4 + h][0:1, :], ones[:, :],
                                 pmb[pr_i][:, :],
                                 start=(t == 0), stop=(t == LS - 1))
                            P.mm(pb[h][:, :],
                                 vsb[bkv][:, h * 128:(h + 1) * 128],
                                 pmb[pr_i][:, :],
                                 start=(t == 0), stop=(t == LS - 1))
                            g_pv = P.mm_done(
                                fence=(pb[h] if t == LS - 1 else None))
                            g_pm_free[pr_i] = g_pv
                            if h == 1:
                                g_kv_free[bkv] = g_pv
                            it += 1
                    g_all_pv = P.inc("sMM", 0)
                    for h in range(2):
                        P.wait("vector", g_all_pv)
                        P.wait("vector", g_rsb_free)
                        P.opc("vector", "sDV",
                              lambda e, s=pb[4 + h]: e.reciprocal(
                                  rsf[:, :], s[0:1, :]))
                        g_r = P.opc("vector", "sDV",
                                    lambda e: e.tensor_copy(rsb[:, :],
                                                            rsf[:, :]))
                        g_sum_free[h] = g_r
                        P.wait("tensor", g_r)
                        P.wait("tensor", g_bc_bank_free)
                        P.mm(pb6[:, :], onesr[:, :], rsb[:, :],
                             start=True, stop=True)
                        g_bc = P.mm_done(fence=pb6)
                        g_rsb_free = g_bc
                        P.wait("vector", g_bc)
                        P.wait("vector", g_bc_free[h])
                        g_cp = P.opc("vector", "sDV",
                                     lambda e, o=bcb[h]: e.tensor_copy(
                                         o[:, :], pb6[:, :]))
                        g_bc_bank_free = g_cp
                        hglob = hg * 2 + h
                        dst = attn[:, hglob * TOK + slot * 512:
                                   hglob * TOK + slot * 512 + 512]
                        g_at = P.op("vector", "sDV",
                                    lambda e, o=dst, a=pb[h], b=bcb[h]:
                                    e.tensor_tensor(o, a[:, :], b[:, :],
                                                    ALU.mult))
                        g_pv_free[h] = g_at
                        g_bc_free[h] = g_at
                    g_q_free[bq] = g_last_scores
                g_msk_free = g_m
            P.barrier()

        if phases >= 5:
            attn = ar(65536, 128, H * TOK, BF16)
            wob = [ar(0 + i * 32768, 128, DT * 512, BF16) for i in range(2)]
            xsbp = [ar(131072 + i * 2048, 128, 512, F32) for i in range(2)]
            osb = [ar(135168 + i * 2048, 128, 512, F32) for i in range(2)]
            g_wo_free = [Z, Z]
            g_x_free = [Z, Z]
            g_bank_free = [Z, Z]
            g_osb_free = [Z, Z]
            for m in range(8):
                bw = m % 2
                P.wait("gpsimd", g_wo_free[bw])
                s_ = wo[:, :, 512 * m:512 * m + 512]
                for q in range(4):
                    g_w = P.dma("gpsimd", "sGL",
                                V3(wob[bw], DT, 512)[:, 8 * q:8 * (q + 1), :],
                                s_[:, 8 * q:8 * (q + 1), :])
                for t in range(8):
                    rot = (m * 8 + t) % 2
                    P.wait("sync", g_x_free[rot])
                    g_x = P.dma("sync", "sLD", xsbp[rot][:, :],
                                x_own[128 * t:128 * (t + 1),
                                      512 * m:512 * m + 512])
                    P.wait("tensor", g_w)
                    P.wait("tensor", g_bank_free[rot])
                    for et in range(DT):
                        P.mm(pb[rot][:, :],
                             attn[:, et * TOK + t * 128:et * TOK + (t + 1) * 128],
                             wob[bw][:, et * 512:(et + 1) * 512],
                             start=(et == 0), stop=(et == DT - 1))
                    g_mm = P.mm_done(fence=pb[rot])
                    P.wait("vector", g_mm)
                    P.wait("vector", g_x)
                    P.wait("vector", g_osb_free[rot])
                    g_o = P.op("vector", "sDV",
                               lambda e, o=osb[rot], a=pb[rot], b=xsbp[rot]:
                               e.tensor_tensor(o[:, :], a[:, :], b[:, :],
                                               ALU.add))
                    g_x_free[rot] = g_o
                    g_bank_free[rot] = g_o
                    P.wait("gpsimd", g_o)
                    g_osb_free[rot] = P.dma(
                        "gpsimd", "sGS",
                        opart[128 * t:128 * (t + 1), 512 * m:512 * m + 512],
                        osb[rot][:, :])
                g_wo_free[bw] = g_mm
            P.barrier()

        if phases >= 6:
            w2b = [ar(0 + i * 1024, 128, 512, BF16) for i in range(2)]
            g6 = [ar(4096 + i * 2048, 128, TOK, BF16) for i in range(2)]
            op6 = [ar(8192 + i * 2048, 128, 512, F32) for i in range(2)]
            fout = [ar(12288 + i * 2048, 128, 512, F32) for i in range(2)]
            g_b2 = P.dma("gpsimd", "sGL", b2sb[:, :], b2t[:, :])
            g_w2_free = [Z, Z]
            g_g6_free = [Z, Z]
            g_op_free = [Z, Z]
            g_fo_free = [Z, Z]
            g_bank_free = [Z] * 8
            for m in range(8):
                for o in range(OT):
                    bw = o % 2
                    P.wait("gpsimd", g_w2_free[bw])
                    g_w = P.dma("gpsimd", "sGL", w2b[bw][:, :],
                                w2t[:, o, 512 * m:512 * m + 512])
                    P.wait("sync", g_g6_free[bw])
                    g_g = P.dma("sync", "sLD", g6[bw][:, :], gbuf[o, :, :])
                    P.wait("tensor", g_w)
                    P.wait("tensor", g_g)
                    if o == 0:
                        P.wait("tensor", g_b2)
                        for t in range(8):
                            P.wait("tensor", g_bank_free[t])
                    for t in range(8):
                        P.mm(pb[t][:, :], g6[bw][:, t * 128:(t + 1) * 128],
                             w2b[bw][:, :], start=(o == 0), stop=False)
                    g_mm = P.mm_done()
                    g_w2_free[bw] = g_mm
                    g_g6_free[bw] = g_mm
                for t in range(8):
                    P.mm(pb[t][:, :], s2sb[:, t * 128:(t + 1) * 128],
                         b2sb[:, 512 * m:512 * m + 512],
                         start=False, stop=True)
                    g_mm = P.mm_done(fence=pb[t])
                    bop = t % 2
                    P.wait("sync", g_op_free[bop])
                    g_op = P.dma("sync", "sLD", op6[bop][:, :],
                                 opart[128 * t:128 * (t + 1),
                                       512 * m:512 * m + 512])
                    P.wait("vector", g_mm)
                    P.wait("vector", g_op)
                    P.wait("vector", g_fo_free[t % 2])
                    g_f = P.op("vector", "sDV",
                               lambda e, o_=fout[t % 2], a=pb[t], b=op6[bop]:
                               e.tensor_tensor(o_[:, :], a[:, :], b[:, :],
                                               ALU.add))
                    g_op_free[bop] = g_f
                    g_bank_free[t] = g_f
                    P.wait("gpsimd", g_f)
                    g_fo_free[t % 2] = P.dma(
                        "gpsimd", "sOUT",
                        out[128 * t:128 * (t + 1), 512 * m:512 * m + 512],
                        fout[t % 2][:, :])

        if attn_tap is not None:
            P.barrier()
            attn_v = ar(65536, 128, H * TOK, BF16)
            P.dma("gpsimd", "sOUT", attn_tap[:, :], attn_v.bitcast(mybir.dt.uint16))
        for t in taps:
            if t == "attn":
                continue
            P.barrier()
            src, dst = srcs[t], tap_outs[t]
            if t == "opart":
                P.dma("gpsimd", "sOUT", dst[:, :], src[:, :])
            else:
                for k in range(src.shape[0]):
                    P.dma("gpsimd", "sOUT", dst[k, :, :],
                          src[k, :, :].bitcast(mybir.dt.uint16))

        P.wait("gpsimd", ("sOUT", P.cnt["sOUT"]))

        @block.sync
        def _(e):
            for fn in P.ops["sync"]:
                fn(e)

        @block.gpsimd
        def _(e):
            for fn in P.ops["gpsimd"]:
                fn(e)

        @block.scalar
        def _(e):
            for fn in P.ops["scalar"]:
                fn(e)

        @block.vector
        def _(e):
            for fn in P.ops["vector"]:
                fn(e)

        @block.tensor
        def _(e):
            for fn in P.ops["tensor"]:
                fn(e)

    return nc


def make_inputs(full):
    x = np.ascontiguousarray(full["L0_attn_hidden_states"])
    wq_full = np.ascontiguousarray(full["L0_attn_wq"])
    wkv = np.ascontiguousarray(full["L0_attn_wkv"])
    wo_full = np.ascontiguousarray(full["L0_attn_wo"])
    def pm(w):
        K_, F_ = w.shape
        return np.ascontiguousarray(
            w.reshape(K_ // 128, 128, F_).transpose(1, 0, 2))

    wq_u = pm(wq_full.reshape(D, H, DQ)[:, :, :DN].reshape(D, H * DN))
    wk_u = pm(np.ascontiguousarray(wkv[:, :H * DN]))
    wv_u = pm(np.ascontiguousarray(wkv[:, H * DN:H * (DN + DVd)]))
    wo_pm = pm(wo_full)
    w1T = pm(np.ascontiguousarray(full["L0_ffn_w1"].T))
    w3T = pm(np.ascontiguousarray(full["L0_ffn_w3"].T))
    w2T = pm(np.ascontiguousarray(full["L0_ffn_w2"].T))
    a1T = pm(np.ascontiguousarray(full["L0_ffn_lora_A1"].T))
    a3T = pm(np.ascontiguousarray(full["L0_ffn_lora_A3"].T))
    a2T = pm(np.ascontiguousarray(full["L0_ffn_lora_A2"].T))
    b1T = np.ascontiguousarray(full["L0_ffn_lora_B1"].T)
    b3T = np.ascontiguousarray(full["L0_ffn_lora_B3"].T)
    b2T = np.ascontiguousarray(full["L0_ffn_lora_B2"].T)

    ident = np.eye(128, dtype=np.float32)
    ones = np.ones((128, 1), dtype=np.float32)
    onesr = np.ones((1, 128), dtype=np.float32)

    in_maps = []
    for c in range(NCORE):
        b, r = divmod(c, 4)
        xs = np.ascontiguousarray(x[b])
        rows = np.r_[512 * r:512 * (r + 1), 512 * (7 - r):512 * (8 - r)]
        xo = np.ascontiguousarray(xs[rows])
        msk = np.zeros((L0 + L1, 128, 512), dtype=np.float32)
        for slot, J in ((0, r), (1, 7 - r)):
            LS = L0 if slot == 0 else L1
            base = 0 if slot == 0 else L0
            qpos = 512 * J + np.arange(512)[None, :]
            for t in range(LS):
                kpos = 128 * t + np.arange(128)[:, None]
                msk[base + t] = (kpos <= qpos).astype(np.float32)
        msk = np.ascontiguousarray(msk.transpose(1, 0, 2))
        in_maps.append({
            "x_seq": xs, "x_own": xo, "wq": wq_u, "wk": wk_u, "wv": wv_u,
            "wo": wo_pm, "w1t": w1T, "w3t": w3T, "w2t": w2T,
            "a1t": a1T, "a3t": a3T, "a2t": a2T,
            "b1t": b1T, "b3t": b3T, "b2t": b2T,
            "mskin": msk, "identin": ident, "onesin": ones, "onesrin": onesr,
        })
    return in_maps


def kernel(**inputs):
    in_maps = make_inputs(inputs)
    nc = build_nc(phases=6)
    res = run_bass_kernel_spmd(nc, in_maps, core_ids=list(range(NCORE)))
    out = np.zeros((B, S, D), dtype=np.float32)
    for c in range(NCORE):
        b, r = divmod(c, 4)
        o = res.results[c]["out"]
        out[b, 512 * r:512 * (r + 1)] = o[:512]
        out[b, 512 * (7 - r):512 * (8 - r)] = o[512:]
    return out
